# version 09
# baseline (speedup 1.0000x reference)
import numpy as np
import concourse.bass as bass
import concourse.mybir as mybir
from concourse.bass_utils import run_bass_kernel_spmd

F32 = mybir.dt.float32
BF16 = mybir.dt.bfloat16
I32 = mybir.dt.int32
AF = mybir.ActivationFunctionType
ALU = mybir.AluOpType
AX = mybir.AxisListType

B, T, C = 4, 8192, 2048
KC = 16
K2 = 32
TCORE = 4096
ST = 1024
SUB = 512
NG = 8
NE = 8
NCD = 4
PI = float(np.pi)
INV2PI = float(1.0 / (2 * np.pi))
TWOPI = float(2 * np.pi)
EPS = 1e-5

ENGS = ("sync", "gpsimd", "tensor", "vector", "scalar")


def build_nc(sups=None, dumps=None, stop=None, xvdump=False, tdebug=False):
    nc = bass.Bass("TRN2", target_bir_lowering=False,
                   dynamic_dma_scratch_size=4096)

    xr = nc.dram_tensor("xr", [TCORE, C], F32, kind="ExternalInput")
    xp = nc.dram_tensor("xp", [TCORE, C], F32, kind="ExternalInput")
    apk = nc.dram_tensor("apk", [KC, 128, K2 * 128], F32, kind="ExternalInput")
    vrpk = nc.dram_tensor("vrpk", [KC, 128, (K2 + KC) * 128], F32, kind="ExternalInput")
    gupk = nc.dram_tensor("gupk", [NE, KC, 128, 2 * KC * 128], F32, kind="ExternalInput")
    dpk = nc.dram_tensor("dpk", [NE, NCD, 128, KC * 512], F32, kind="ExternalInput")
    wrk = nc.dram_tensor("wrk", [128, KC * 8], F32, kind="ExternalInput")
    eye = nc.dram_tensor("eye", [128, 128], F32, kind="ExternalInput")
    selv = nc.dram_tensor("selv", [128, 1], F32, kind="ExternalInput")
    nselv = nc.dram_tensor("nselv", [128, 1], F32, kind="ExternalInput")
    nselbigv = nc.dram_tensor("nselbig", [128, 1], F32, kind="ExternalInput")
    out = nc.dram_tensor("out", [TCORE, C], F32, kind="ExternalOutput")

    pad = nc.alloc_sbuf_tensor("ringpad", [128, 12288], mybir.dt.uint8)
    acc = nc.alloc_sbuf_tensor("acc", [128, NG * C], F32)
    xnT = nc.alloc_sbuf_tensor("xnT", [128, KC * (ST + 1)], BF16)
    xst = nc.alloc_sbuf_tensor("xst", [128, C], F32)
    kvp = nc.alloc_sbuf_tensor("kvp", [128, KC * ST], BF16)
    war = nc.alloc_sbuf_tensor("war", [128, 24576], BF16)
    scr = nc.alloc_sbuf_tensor("scr", [128, 4096], F32)
    wrs = nc.alloc_sbuf_tensor("wrs", [128, KC * 8], BF16)
    idb = nc.alloc_sbuf_tensor("idb", [128, 128], BF16)
    idf = nc.alloc_sbuf_tensor("idf", [128, 128], F32)
    wmap = nc.alloc_sbuf_tensor("wmap", [128, NG * NE], F32)
    lgt = nc.alloc_sbuf_tensor("lgt", [128, 16], F32)
    lgt8 = nc.alloc_sbuf_tensor("lgt8", [128, NG * 8], F32)
    carr = nc.alloc_sbuf_tensor("carr", [128, 24], F32)
    col = nc.alloc_sbuf_tensor("col", [128, 16], F32)
    sel = nc.alloc_sbuf_tensor("sel", [128, 1], F32)
    nsel = nc.alloc_sbuf_tensor("nsel", [128, 1], F32)
    nselbig = nc.alloc_sbuf_tensor("nselb", [128, 1], F32)
    scrb = scr.bitcast(BF16)
    scri = scr.bitcast(I32)

    def S(k):
        return scr[:, 512 * k:512 * (k + 1)]

    def Si(k):
        return scri[:, 512 * k:512 * (k + 1)]

    cneg1 = col[:, 0:1]
    chalf = col[:, 13:14]
    zc = col[:, 1:2]
    stdc = col[:, 2:3]
    rstdc = col[:, 3:4]
    meanc = col[:, 4:5]
    sumc = col[:, 5:6]
    sqc = col[:, 6:7]
    varc = col[:, 7:8]
    m1c = col[:, 8:9]
    m2c = col[:, 9:10]
    p1c = col[:, 10:11]
    p2c = col[:, 11:12]
    epsc = col[:, 12:13]

    pb = [nc.alloc_psum_tensor(f"pb{i}", [128, 512], F32) for i in range(4)]
    TB = [nc.alloc_psum_tensor(f"ptb{i}", [128, 128], BF16) for i in range(4)]

    def aslab(i):
        return war[:, 4096 * i:4096 * (i + 1)]

    def vrslab(i):
        return war[:, 12288 + 6144 * i:12288 + 6144 * (i + 1)]

    def guslab(i):
        return war[:, 4096 * i:4096 * (i + 1)]

    def dslab(i):
        return war[:, 8192 + 8192 * i:8192 + 8192 * (i + 1)]

    q = {e: [] for e in ENGS}
    cnt = {"XL": 0, "WL": 0, "TP": 0, "VE": 0, "SE": 0, "OS": 0, "GE": 0}
    lastw = {e: {s: 0 for s in cnt} for e in ENGS}
    SEQ = {}
    carry_inc = {}
    bankfree = {i: ("VE", 0) for i in range(4)}
    tbfree = {i: 0 for i in range(4)}

    def emit(eng, fn):
        q[eng].append(fn)

    def wait(eng, sname, val):
        if val <= 0 or lastw[eng][sname] >= val:
            return
        lastw[eng][sname] = val
        emit(eng, lambda e, sn=sname, v=val: e.wait_ge(SEM[sn], v))

    def step(eng, sname, fn, k=1):
        cnt[sname] += k
        emit(eng, lambda e, sn=sname, f=fn, kk=k: f(e).then_inc(SEM[sn], kk))
        return cnt[sname]

    def plain(eng, fn):
        emit(eng, fn)

    def wait_bank(eng, b):
        sname, v = bankfree[b]
        wait(eng, sname, v)

    plain("vector", lambda e: e.memset(col.ap(), 0.0))
    plain("vector", lambda e: e.memset(cneg1, -1.0))
    plain("vector", lambda e: e.memset(epsc, EPS))
    plain("vector", lambda e: e.memset(chalf, 0.5))
    plain("vector", lambda e: e.memset(xnT.ap(), 0.0))
    plain("vector", lambda e: e.memset(carr[:, 0:8], -1e30))
    plain("vector", lambda e: e.memset(carr[:, 8:16], 0.0))
    step("vector", "VE", lambda e: e.memset(carr[:, 16:24], 0.0))
    step("sync", "XL", lambda e: e.dma_start(idf.ap(), eye.ap()), 16)
    step("sync", "XL", lambda e: e.dma_start(sel.ap(), selv.ap()), 16)
    step("sync", "XL", lambda e: e.dma_start(nsel.ap(), nselv.ap()), 16)
    x0 = step("sync", "XL", lambda e: e.dma_start(nselbig.ap(), nselbigv.ap()), 16)
    step("gpsimd", "WL", lambda e: e.dma_start(idb.ap(), eye.ap()), 16)
    w0 = step("gpsimd", "WL", lambda e: e.dma_start(wrs.ap(), wrk.ap()), 16)
    wait("vector", "XL", x0)
    wait("vector", "WL", w0)
    wait("tensor", "WL", w0)
    wait("tensor", "XL", x0)

    def ln_phase(s):
        real = s >= 4
        xsrc = xr if real else xp
        tb = (s % 4) * ST
        xnst = scrb[:, 0:2048]
        for g in range(NG):
            wait("sync", "VE", SEQ.get("xfree_ve", 0))
            wait("sync", "SE", SEQ.get("xfree_se", 0))
            xl = step("sync", "XL", lambda e, o=(tb + g * 128) * C:
                      e.dma_start(xst.ap(),
                                  bass.AP(xsrc, o, [[C, 128], [1, C]])), 16)
            wait("vector", "XL", xl)
            vr = step("vector", "VE", lambda e: e.reduce_sum(sumc, xst.ap(), AX.X))
            wait("vector", "VE", vr)
            wait("scalar", "XL", xl)
            wait("scalar", "VE", SEQ.get("scr_free_ve", 0))
            wait("scalar", "GE", SEQ.get("scr_free_ge", 0))
            wait("scalar", "VE", SEQ.get("sq_free_ve", 0))
            wait("scalar", "TP", SEQ.get("xnst_free_tp", 0))
            se1 = step("scalar", "SE", lambda e, ss=xnst:
                       e.activation(ss, xst.ap(), AF.Square, accum_out=sqc))
            SEQ["xfree_se"] = se1
            wait("vector", "SE", se1)
            plain("vector", lambda e: e.tensor_scalar(meanc, sumc, 1.0 / C, None, ALU.mult))
            plain("vector", lambda e: e.tensor_scalar(varc, sqc, 1.0 / C, None, ALU.mult))
            vm = step("vector", "VE", lambda e: e.tensor_tensor(m1c, meanc, meanc, ALU.mult))
            wait("vector", "VE", vm)
            v1 = step("vector", "VE", lambda e:
                      e.tensor_tensor(varc, varc, m1c, ALU.subtract))
            SEQ["sq_free_ve"] = v1
            wait("scalar", "VE", v1)
            se2 = step("scalar", "SE", lambda e:
                       e.activation(stdc, varc, AF.Sqrt, bias=epsc))
            wait("vector", "SE", se2)
            vrc = step("vector", "VE", lambda e: e.reciprocal(rstdc, stdc))
            wait("vector", "VE", vrc)
            wait("vector", "TP", SEQ.get("xnst_free_tp", 0))
            vxn = step("vector", "VE", lambda e, ss=xnst:
                       e.tensor_scalar(ss, xst.ap(), meanc, rstdc,
                                       ALU.subtract, ALU.mult))
            if real:
                vac = step("vector", "VE", lambda e, gg=g:
                           e.tensor_copy(acc[:, gg * C:(gg + 1) * C], xst.ap()))
                SEQ["xfree_ve"] = vac
            else:
                SEQ["xfree_ve"] = vxn
            for k in range(KC):
                bk = k % 4
                wait("tensor", "VE", vxn)
                wait("tensor", "VE", tbfree[bk])
                tp = step("tensor", "TP", lambda e, kk=k, bb=bk, ss=xnst:
                          e.transpose(TB[bb].ap(),
                                      ss[:, kk * 128:(kk + 1) * 128], idb.ap()))
                wait("vector", "TP", tp)
                vf = step("vector", "VE", lambda e, kk=k, bb=bk, gg=g:
                          e.tensor_copy(
                              xnT[:, kk * (ST + 1) + 1 + gg * 128:
                                  kk * (ST + 1) + 1 + gg * 128 + 128],
                              TB[bb].ap()))
                tbfree[bk] = vf
            SEQ["xnst_free_tp"] = cnt["TP"]
            if g == 3:
                SEQ["xnT_half_ve"] = cnt["VE"]
        SEQ["xnT_ve"] = cnt["VE"]

    def halo_copy():
        src = bass.AP(xnT, ST, [[KC * (ST + 1), 128], [(ST + 1), KC]])
        dst = bass.AP(xnT, 0, [[KC * (ST + 1), 128], [(ST + 1), KC]])
        wait("vector", "TP", SEQ.get("xnT_lastuse_tp", 0))
        step("vector", "VE", lambda e, s_=src, d_=dst: e.tensor_copy(d_, s_))

    def attn_scan_phase(s):
        real = s >= 4
        mcs = []
        for p in range(8):
            mcs += [p, p + 8]
        def issue_slab(i):
            wait("gpsimd", "TP", max(SEQ.get(f"asl{i % 2}_free", 0),
                                     SEQ.get("GU_last", 0)))
            return step("gpsimd", "WL", lambda e, m=mcs[i], b=i % 2:
                        e.dma_start(aslab(b), apk[m]), 16)

        wls = [None] * 16
        wls[0] = issue_slab(0)
        wls[1] = issue_slab(1)
        fi = 0
        for ip in range(8):
            p = ip
            for ih in range(2):
                i = 2 * ip + ih
                buf = i % 2
                for sub in range(2):
                    bank = fi % 4
                    wait("tensor", "WL", wls[i])
                    wait("tensor", "VE",
                         SEQ["xnT_half_ve"] if sub == 0 else SEQ["xnT_ve"])
                    wait_bank("tensor", bank)

                    def fill(e, bk=bank, bf=buf, sb=sub):
                        ins = None
                        for k2 in range(K2):
                            blk = k2 % KC
                            off = 0 if k2 < KC else 1
                            rhs = xnT[:, blk * (ST + 1) + off + sb * SUB:
                                      blk * (ST + 1) + off + sb * SUB + SUB]
                            ins = e.matmul(pb[bk].ap(),
                                           aslab(bf)[:, k2 * 128:(k2 + 1) * 128],
                                           rhs, start=(k2 == 0), stop=(k2 == K2 - 1))
                        return ins
                    tpf = step("tensor", "TP", fill)
                    slot = sub * 2 + ih
                    wait("scalar", "TP", tpf)
                    wait("scalar", "VE", SEQ.get(f"slot{slot}_free", 0))
                    wait("scalar", "GE", SEQ.get(f"slot{slot}_free_ge", 0))
                    ses0 = step("scalar", "SE", lambda e, bk=bank, sl=slot:
                                e.activation(S(sl), pb[bk].ap(), AF.Sigmoid))
                    bankfree[bank] = ("SE", ses0)
                    wait("vector", "SE", ses0)
                    ses = step("vector", "VE", lambda e, sl=slot:
                               e.tensor_scalar(S(sl), S(sl), 0.5, None, ALU.max))
                    if xvdump:
                        wait("vector", "SE", ses)
                        mcv = mcs[i]
                        step("vector", "VE", lambda e, sl=slot, m=mcv, sb=sub:
                             e.tensor_copy(
                                 kvp[:, m * ST + sb * SUB:m * ST + sb * SUB + SUB],
                                 S(sl)))
                    SEQ[f"sig{slot}_ve"] = ses
                    fi += 1
                SEQ[f"asl{buf}_free"] = cnt["TP"]
                if i + 2 < 16:
                    wls[i + 2] = issue_slab(i + 2)
            for sub in range(2):
                re_ = S(sub * 2 + 0)
                im_ = S(sub * 2 + 1)
                cm = carr[:, p:p + 1]
                cP = carr[:, 8 + p:8 + p + 1]
                cA = carr[:, 16 + p:16 + p + 1]
                wait("gpsimd", "VE", SEQ[f"sig{sub * 2}_ve"])
                wait("gpsimd", "VE", SEQ[f"sig{sub * 2 + 1}_ve"])
                wait("gpsimd", "VE", SEQ.get("s45_scan_ve", 0))
                plain("gpsimd", lambda e, r=re_: e.tensor_tensor(S(4), r, r, ALU.mult))
                plain("gpsimd", lambda e, m=im_: e.tensor_tensor(S(5), m, m, ALU.mult))
                g2m = step("gpsimd", "GE", lambda e:
                           e.tensor_tensor(S(4), S(4), S(5), ALU.add))
                wait("scalar", "GE", g2m)
                plain("scalar", lambda e, m=im_: e.activation(S(5), m, AF.Ln))
                seL = step("scalar", "SE", lambda e, r=re_: e.activation(r, r, AF.Ln))
                wait("gpsimd", "SE", seL)
                gdr = step("gpsimd", "GE", lambda e, r=re_:
                           e.tensor_tensor(S(5), S(5), r, ALU.subtract))
                se3 = step("scalar", "SE", lambda e: e.activation(S(6), S(4), AF.Ln))
                wait("scalar", "GE", gdr)
                seR = step("scalar", "SE", lambda e, m=im_:
                           e.activation(m, S(5), AF.Exp))
                wait("vector", "SE", se3)
                wait("vector", "VE", carry_inc.get(("m", p), 0))
                vs1 = step("vector", "VE", lambda e, c=cm:
                      e.tensor_tensor_scan(S(4), S(6), S(6), c, ALU.max, ALU.bypass))
                wait("vector", "VE", vs1)
                carry_inc[("m", p)] = step("vector", "VE", lambda e, c=cm:
                      e.tensor_copy(c, S(4)[:, SUB - 1:SUB]))
                v2 = step("vector", "VE", lambda e:
                          e.tensor_tensor(S(6), S(6), S(4), ALU.subtract))
                wait("vector", "VE", carry_inc.get(("P", p), 0))
                vs2 = step("vector", "VE", lambda e, c=cP:
                      e.tensor_tensor_scan(S(4), S(6), S(6), c, ALU.add, ALU.bypass))
                wait("vector", "VE", vs2)
                carry_inc[("P", p)] = step("vector", "VE", lambda e, c=cP:
                      e.tensor_copy(c, S(4)[:, SUB - 1:SUB]))
                if real and not xvdump:
                    wait("scalar", "VE", vs2)
                    seP = step("scalar", "SE", lambda e, r=re_:
                               e.activation(r, S(4), AF.Exp, scale=chalf))
                se4 = step("scalar", "SE", lambda e, m=im_:
                           e.activation(m, m, AF.Arctan))
                wait("vector", "SE", se4)
                wait("vector", "VE", carry_inc.get(("A", p), 0))
                vs3 = step("vector", "VE", lambda e, m=im_, c=cA:
                      e.tensor_tensor_scan(S(6), m, m, c, ALU.add, ALU.bypass))
                wait("vector", "VE", vs3)
                vA = step("vector", "VE", lambda e, c=cA:
                          e.tensor_copy(c, S(6)[:, SUB - 1:SUB]))
                carry_inc[("A", p)] = vA
                SEQ["s45_scan_ve"] = vA
                SEQ[f"slot{sub * 2}_free"] = vA
                SEQ[f"slot{sub * 2 + 1}_free"] = vA
                if real and not xvdump:
                    plain("vector", lambda e:
                          e.tensor_scalar(S(5), S(6), INV2PI, None, ALU.mult))
                    plain("vector", lambda e: e.tensor_copy(Si(7), S(5)))
                    v4 = step("vector", "VE", lambda e:
                              e.scalar_tensor_tensor(S(5), Si(7), TWOPI, S(6),
                                                     ALU.mult, ALU.subtract))
                    wait("scalar", "VE", v4)
                    se5 = step("scalar", "SE", lambda e:
                               e.activation(S(5), S(5), AF.Sin, scale=cneg1))
                    wait("gpsimd", "SE", max(se5, seP))
                    gki = step("gpsimd", "GE", lambda e, pp=p, sb=sub, r=re_:
                               e.tensor_tensor(
                                   kvp[:, (8 + pp) * ST + sb * SUB:
                                       (8 + pp) * ST + sb * SUB + SUB],
                                   r, S(5), ALU.mult))
                    wait("vector", "GE", gki)
                    plain("vector", lambda e:
                          e.tensor_scalar(S(6), S(6), PI / 2, None, ALU.add))
                    plain("vector", lambda e:
                          e.tensor_scalar(S(5), S(6), INV2PI, None, ALU.mult))
                    plain("vector", lambda e: e.tensor_copy(Si(7), S(5)))
                    v5 = step("vector", "VE", lambda e:
                              e.scalar_tensor_tensor(S(5), Si(7), TWOPI, S(6),
                                                     ALU.mult, ALU.subtract))
                    wait("scalar", "VE", v5)
                    se6 = step("scalar", "SE", lambda e:
                               e.activation(S(5), S(5), AF.Sin, scale=cneg1))
                    wait("gpsimd", "SE", se6)
                    gkr = step("gpsimd", "GE", lambda e, pp=p, sb=sub, r=re_:
                               e.tensor_tensor(
                                   kvp[:, pp * ST + sb * SUB:pp * ST + sb * SUB + SUB],
                                   r, S(5), ALU.mult))
                    SEQ["kv_ge"] = gkr
                    SEQ[f"slot{sub * 2}_free"] = cnt["VE"]
                    SEQ[f"slot{sub * 2}_free_ge"] = gkr
                    SEQ[f"slot{sub * 2 + 1}_free_ge"] = gkr
        if s >= 4:
            SEQ["vr_wl0"] = issue_vr(0)
            SEQ["vr_wl1"] = issue_vr(1)
        SEQ["kv_ve"] = cnt["VE"]
        SEQ["kv_ge_all"] = cnt["GE"]
        SEQ["scr_free_ve"] = cnt["VE"]
        SEQ["scr_free_ge"] = cnt["GE"]
        SEQ["A_last"] = cnt["TP"]
        SEQ["xnT_lastuse_tp"] = cnt["TP"]

    def carry_fixup():
        nbb = nselbig.broadcast_to((128, 8))
        wait("vector", "VE", max(carry_inc.values(), default=0))
        plain("vector", lambda e, n=nbb:
              e.scalar_tensor_tensor(carr[:, 0:8], carr[:, 0:8], sel.ap(), n,
                                     ALU.mult, ALU.add))
        plain("vector", lambda e:
              e.tensor_scalar(carr[:, 8:16], carr[:, 8:16], sel.ap(), None, ALU.mult))
        vcf = step("vector", "VE", lambda e:
              e.tensor_scalar(carr[:, 16:24], carr[:, 16:24], sel.ap(), None, ALU.mult))
        for k in list(carry_inc):
            carry_inc[k] = vcf

    def issue_vr(i):
        wait("gpsimd", "TP", max(SEQ.get(f"vrs{i % 2}_free", 0),
                                 SEQ.get("D_last", 0)))
        return step("gpsimd", "WL", lambda e, m=i, b=i % 2:
                    e.dma_start(vrslab(b), vrpk[m]), 16)

    def attn_tail_phase(s):
        wl = [None] * KC
        wl[0] = SEQ["vr_wl0"]
        wl[1] = SEQ["vr_wl1"]
        pend = None

        def epilogue(bval, brr, sub, mc, tpr):
            wait("scalar", "TP", tpr)
            wait("scalar", "VE", SEQ.get("s45_free_ve", 0))
            wait("scalar", "GE", SEQ.get("scr_free_ge", 0))
            plain("scalar", lambda e, bk=brr: e.activation(S(4), pb[bk].ap(), AF.Relu))
            plain("scalar", lambda e: e.activation(S(4), S(4), AF.Sigmoid))
            plain("scalar", lambda e: e.activation(S(4), S(4), AF.Square))
            ser = step("scalar", "SE", lambda e: e.activation(S(4), S(4), AF.Square))
            bankfree[brr] = ("SE", ser)
            t2b = scrb[:, 5120:5632]
            wait("vector", "SE", ser)
            vt2 = step("vector", "VE", lambda e, bk=bval, tb2=t2b:
                       e.tensor_tensor(tb2, pb[bk].ap(), S(4), ALU.mult))
            bankfree[bval] = ("VE", vt2)
            SEQ["s45_free_ve"] = vt2
            tpts = []
            for tg in range(4):
                tb_ = tg % 4
                wait("tensor", "VE", vt2)
                wait("tensor", "VE", tbfree[tb_])
                tpt = step("tensor", "TP", lambda e, t=tg, b=tb_, tb2=t2b:
                           e.transpose(TB[b].ap(),
                                       tb2[:, t * 128:(t + 1) * 128], idb.ap()))
                tpts.append(tpt)
            wait("vector", "TP", tpts[-1])
            for tg in range(4):
                tb_ = tg % 4
                G = sub * 4 + tg
                vfa = step("vector", "VE", lambda e, b=tb_, gg=G, m=mc:
                           e.tensor_tensor(
                               acc[:, gg * C + m * 128:gg * C + m * 128 + 128],
                               acc[:, gg * C + m * 128:gg * C + m * 128 + 128],
                               TB[b].ap(), ALU.add))
                tbfree[tb_] = vfa

        for mc in range(KC):
            buf = mc % 2
            for sub in range(2):
                it = 2 * mc + sub
                bval = (2 * it) % 4
                brr = (2 * it + 1) % 4
                wait("tensor", "WL", wl[mc])
                wait("tensor", "VE", SEQ["kv_ve"])
                wait("tensor", "GE", SEQ.get("kv_ge_all", 0))
                wait_bank("tensor", bval)
                wait_bank("tensor", brr)

                def fillv(e, bf=buf, sb=sub, bk=bval):
                    ins = None
                    for kc in range(KC):
                        rhs = kvp[:, kc * ST + sb * SUB:kc * ST + sb * SUB + SUB]
                        ins = e.matmul(pb[bk].ap(),
                                       vrslab(bf)[:, (K2 + kc) * 128:(K2 + kc + 1) * 128],
                                       rhs, start=(kc == 0), stop=(kc == KC - 1))
                    return ins
                step("tensor", "TP", fillv)

                def fillr(e, bf=buf, sb=sub, bk=brr):
                    ins = None
                    for k2 in range(K2):
                        blk = k2 % KC
                        off = 0 if k2 < KC else 1
                        rhs = xnT[:, blk * (ST + 1) + off + sb * SUB:
                                  blk * (ST + 1) + off + sb * SUB + SUB]
                        ins = e.matmul(pb[bk].ap(),
                                       vrslab(bf)[:, k2 * 128:(k2 + 1) * 128],
                                       rhs, start=(k2 == 0), stop=(k2 == K2 - 1))
                    return ins
                tprr = step("tensor", "TP", fillr)
                if pend is not None:
                    epilogue(*pend)
                pend = (bval, brr, sub, mc, tprr)
            SEQ[f"vrs{buf}_free"] = cnt["TP"]
            if mc + 2 < KC:
                wl[mc + 2] = issue_vr(mc + 2)
        epilogue(*pend)
        SEQ["VR_last"] = cnt["TP"]
        SEQ["kvdead_tp"] = cnt["TP"]
        SEQ["scr_free_ve"] = cnt["VE"]

    def router_fill_phase(s):
        for tg in range(NG):
            bank = tg % 2
            wait("tensor", "VE", SEQ["xnT_ve"])
            wait_bank("tensor", bank)

            def fillrt(e, t=tg, bk=bank):
                ins = None
                for kc in range(KC):
                    lhsT = xnT[:, kc * (ST + 1) + 1 + t * 128:
                               kc * (ST + 1) + 1 + t * 128 + 128]
                    ins = e.matmul(pb[bk][:, 0:8], lhsT,
                                   wrs[:, kc * 8:(kc + 1) * 8],
                                   start=(kc == 0), stop=(kc == KC - 1))
                return ins
            tprt = step("tensor", "TP", fillrt)
            wait("vector", "TP", tprt)
            vl = step("vector", "VE", lambda e, bk=bank, t=tg:
                      e.tensor_copy(lgt8[:, t * 8:(t + 1) * 8], pb[bk][:, 0:8]))
            bankfree[bank] = ("VE", vl)
            SEQ["lgt8_ve"] = vl

    def router_top2_phase(s):
        wait("vector", "VE", SEQ["lgt8_ve"])
        for tg in range(NG):
            g1 = step("vector", "VE", lambda e, t=tg:
                      e.reduce_max(m1c, lgt8[:, t * 8:(t + 1) * 8], AX.X))
            wait("vector", "VE", g1)
            g2 = step("vector", "VE", lambda e, t=tg:
                  e.tensor_scalar(lgt[:, 8:16], lgt8[:, t * 8:(t + 1) * 8], m1c, None, ALU.is_ge))
            wait("vector", "VE", g2)
            g3 = step("vector", "VE", lambda e, t=tg:
                  e.scalar_tensor_tensor(S(4)[:, 0:8], lgt[:, 8:16], -1e30,
                                         lgt8[:, t * 8:(t + 1) * 8], ALU.mult, ALU.add))
            wait("vector", "VE", g3)
            g4 = step("vector", "VE", lambda e: e.reduce_max(m2c, S(4)[:, 0:8], AX.X))
            wait("vector", "VE", g4)
            g5 = step("vector", "VE", lambda e:
                  e.tensor_scalar(S(4)[:, 8:16], S(4)[:, 0:8], m2c, None, ALU.is_ge))
            v6 = step("vector", "VE", lambda e:
                      e.tensor_tensor(m2c, m2c, m1c, ALU.subtract))
            wait("scalar", "VE", v6)
            se7 = step("scalar", "SE", lambda e:
                       e.activation(p2c, m2c, AF.Sigmoid))
            wait("vector", "SE", se7)
            g6 = step("vector", "VE", lambda e:
                  e.tensor_scalar(p1c, p2c, -1.0, 1.0, ALU.mult, ALU.add))
            wait("vector", "VE", g6)
            g7 = step("vector", "VE", lambda e:
                  e.tensor_scalar(lgt[:, 8:16], lgt[:, 8:16], p1c, None, ALU.mult))
            wait("vector", "VE", max(g5, g7))
            step("vector", "VE", lambda e, t=tg:
                 e.scalar_tensor_tensor(wmap[:, t * NE:(t + 1) * NE], S(4)[:, 8:16],
                                        p2c, lgt[:, 8:16], ALU.mult, ALU.add))
        SEQ["wmap_ve"] = cnt["VE"]
        SEQ["scr_free_ve"] = cnt["VE"]

    def ffn_phase(s):
        for e_ in range(NE):
            def issue_gu(i, ee):
                wait("gpsimd", "TP", max(SEQ.get(f"gus{i % 2}_free", 0),
                                         SEQ.get("A_last", 0)))
                return step("gpsimd", "WL", lambda en, m=i, b=i % 2, x=ee:
                            en.dma_start(guslab(b), gupk[x, m]), 16)

            wl = [None] * KC
            wl[0] = issue_gu(0, e_)
            wl[1] = issue_gu(1, e_)
            for mc in range(KC):
                buf = mc % 2
                for half in range(2):
                    bg = 2 * half
                    bu = 2 * half + 1
                    wait("tensor", "WL", wl[mc])
                    wait_bank("tensor", bg)
                    wait_bank("tensor", bu)

                    def fillgu(en, bf=buf, hh=half, bk=bg, off=0):
                        ins = None
                        for kc in range(KC):
                            rhs = xnT[:, kc * (ST + 1) + 1 + hh * SUB:
                                      kc * (ST + 1) + 1 + hh * SUB + SUB]
                            ins = en.matmul(pb[bk].ap(),
                                            guslab(bf)[:, off + kc * 128:
                                                        off + (kc + 1) * 128],
                                            rhs, start=(kc == 0), stop=(kc == KC - 1))
                        return ins
                    step("tensor", "TP",
                         lambda en, bf=buf, hh=half, bk=bg: fillgu(en, bf, hh, bk, 0))
                    tpu = step("tensor", "TP",
                               lambda en, bf=buf, hh=half, bk=bu:
                               fillgu(en, bf, hh, bk, KC * 128))
                    wait("scalar", "TP", tpu)
                    wait("scalar", "VE", SEQ.get("s45_free_ve", 0))
                    wait("scalar", "GE", SEQ.get("scr_free_ge", 0))
                    seg = step("scalar", "SE", lambda en, bk=bg:
                               en.activation(S(4), pb[bk].ap(), AF.Silu))
                    bankfree[bg] = ("SE", seg)
                    wait("vector", "SE", seg)
                    if e_ == 0 and mc == 0 and half == 0:
                        wait("vector", "TP", SEQ.get("kvdead_tp", 0))
                    plain("vector", lambda en:
                          en.tensor_scalar(S(4), S(4), 7.0, None, ALU.min))
                    vp = step("vector", "VE", lambda en, m=mc, hh=half, bk=bu:
                              en.tensor_tensor(
                                  kvp[:, m * ST + hh * SUB:m * ST + hh * SUB + SUB],
                                  S(4), pb[bk].ap(), ALU.mult))
                    bankfree[bu] = ("VE", vp)
                    SEQ["s45_free_ve"] = vp
                SEQ[f"gus{buf}_free"] = cnt["TP"]
                if mc + 2 < KC:
                    wl[mc + 2] = issue_gu(mc + 2, e_)
            SEQ["pT_ve"] = cnt["VE"]
            SEQ["GU_last"] = cnt["TP"]

            def issue_d(i, ee):
                wait("gpsimd", "TP",
                     max(SEQ.get("VR_last", 0), SEQ.get(f"ds{i % 2}_free", 0)))
                return step("gpsimd", "WL", lambda en, m=i, b=i % 2, x=ee:
                            en.dma_start(dslab(b), dpk[x, m]), 16)

            wld = [None] * NCD
            wld[0] = issue_d(0, e_)
            wld[1] = issue_d(1, e_)
            for ncd in range(NCD):
                buf = ncd % 2
                for tg in range(NG):
                    bank = (ncd * NG + tg) % 4
                    wait("tensor", "WL", wld[ncd])
                    wait("tensor", "VE", SEQ["pT_ve"])
                    wait_bank("tensor", bank)

                    def filld(en, t=tg, bf=buf, bk=bank):
                        ins = None
                        for kc in range(KC):
                            lhsT = kvp[:, kc * ST + t * 128:kc * ST + t * 128 + 128]
                            ins = en.matmul(pb[bk].ap(), lhsT,
                                            dslab(bf)[:, kc * 512:(kc + 1) * 512],
                                            start=(kc == 0), stop=(kc == KC - 1))
                        return ins
                    tpd = step("tensor", "TP", filld)
                    wait("vector", "TP", tpd)
                    wcol = wmap[:, tg * NE + e_:tg * NE + e_ + 1]
                    ao = tg * C + ncd * 512
                    vd = step("vector", "VE", lambda en, bk=bank, w=wcol, a=ao:
                              en.scalar_tensor_tensor(
                                  acc[:, a:a + 512], pb[bk].ap(), w,
                                  acc[:, a:a + 512], ALU.mult, ALU.add))
                    bankfree[bank] = ("VE", vd)
                SEQ[f"ds{buf}_free"] = cnt["TP"]
                if ncd + 2 < NCD:
                    wld[ncd + 2] = issue_d(ncd + 2, e_)
            SEQ["D_last"] = cnt["TP"]
        SEQ["acc_ve"] = cnt["VE"]
        SEQ["xnT_lastuse_tp"] = cnt["TP"]
        SEQ["scr_free_ve"] = cnt["VE"]

    def store_phase(s):
        tb = (s - 4) * ST
        wait("sync", "VE", SEQ["acc_ve"])
        os = step("sync", "OS", lambda e:
                  e.dma_start(bass.AP(out, tb * C, [[C, 128], [128 * C, NG], [1, C]]),
                              acc.ap()), 16)
        SEQ["store_os"] = os

    if sups is None:
        sups = list(range(8))
    first = True
    for s in sups:
        if not first:
            halo_copy()
        first = False
        if s >= 4:
            wait("vector", "OS", SEQ.get("store_os", 0))
        ln_phase(s)
        if stop == "ln":
            continue
        if s >= 4:
            router_fill_phase(s)
        attn_scan_phase(s)
        if s == 3:
            carry_fixup()
        if stop == "scan":
            continue
        if s >= 4:
            attn_tail_phase(s)
            if stop == "tail":
                continue
            router_top2_phase(s)
            if stop == "router":
                continue
            ffn_phase(s)
            store_phase(s)
    if dumps:
        dmap = {"xnT": xnT, "acc": acc, "kvp": kvp, "wmap": wmap,
                "carr": carr, "scr": scr, "col": col, "lgt": lgt}
        step("vector", "VE", lambda e: e.memset(col[:, 15:16], 0.0))
        step("scalar", "SE", lambda e: e.activation(col[:, 14:15], col[:, 15:16], AF.Square))
        wait("sync", "VE", cnt["VE"])
        wait("sync", "SE", cnt["SE"])
        wait("sync", "TP", cnt["TP"])
        for nm in dumps:
            h = dmap[nm]
            d = nc.dram_tensor("d_" + nm, list(h.shape), h.dtype, kind="ExternalOutput")
            step("sync", "OS", lambda e, dd=d, a=h.ap(): e.dma_start(dd.ap(), a), 16)
    wait("sync", "OS", cnt["OS"])

    global SEM
    with (
        nc.semaphore("sXL") as s_xl,
        nc.semaphore("sWL") as s_wl,
        nc.semaphore("sTP") as s_tp,
        nc.semaphore("sVE") as s_ve,
        nc.semaphore("sSE") as s_se,
        nc.semaphore("sOS") as s_os,
        nc.semaphore("sGE") as s_ge,
        nc.Block() as block,
    ):
        SEM = {"XL": s_xl, "WL": s_wl, "TP": s_tp, "VE": s_ve, "SE": s_se, "OS": s_os,
               "GE": s_ge}

        @block.sync
        def _(e):
            for fn in q["sync"]:
                fn(e)

        @block.gpsimd
        def _(e):
            for fn in q["gpsimd"]:
                fn(e)

        @block.tensor
        def _(e):
            for fn in q["tensor"]:
                fn(e)

        @block.vector
        def _(e):
            for fn in q["vector"]:
                fn(e)

        @block.scalar
        def _(e):
            for fn in q["scalar"]:
                fn(e)

    return nc


def _host_prep(W_act, W_val, W_rec, Wr, Wg, Wu, Wd):
    perm = np.concatenate([np.arange(0, C, 2), np.arange(1, C, 2)])

    def mcol_pack(w):
        K = w.shape[0]
        nmc = w.shape[1] // 128
        r = w.reshape(K // 128, 128, nmc, 128)
        r = r.transpose(2, 1, 0, 3).reshape(nmc, 128, K)
        return np.ascontiguousarray(r)

    A = mcol_pack(np.ascontiguousarray(W_act[perm, :].T))
    R = mcol_pack(np.ascontiguousarray(W_rec.T))
    V = mcol_pack(np.ascontiguousarray(W_val[:, perm].T))
    vr = np.concatenate([R, V], axis=2)

    G = np.stack([mcol_pack(Wg[e]) for e in range(NE)])
    U = np.stack([mcol_pack(Wu[e]) for e in range(NE)])
    gu = np.concatenate([G, U], axis=3)

    Wd5 = Wd.reshape(NE, KC, 128, NCD, 512)
    dp = np.ascontiguousarray(
        Wd5.transpose(0, 3, 2, 1, 4).reshape(NE, NCD, 128, KC * 512))

    wr = np.ascontiguousarray(
        Wr.reshape(KC, 128, NE).transpose(1, 0, 2).reshape(128, KC * NE))

    return dict(apk=A, vrpk=vr, gupk=gu, dpk=dp, wrk=wr)


def kernel(L0_attn_x, L0_attn_W_act, L0_attn_W_val, L0_attn_W_rec,
           L0_ffn_Wr, L0_ffn_Wg, L0_ffn_Wu, L0_ffn_Wd):
    x = np.ascontiguousarray(np.asarray(L0_attn_x, dtype=np.float32))
    w = _host_prep(np.asarray(L0_attn_W_act, np.float32),
                   np.asarray(L0_attn_W_val, np.float32),
                   np.asarray(L0_attn_W_rec, np.float32),
                   np.asarray(L0_ffn_Wr, np.float32),
                   np.asarray(L0_ffn_Wg, np.float32),
                   np.asarray(L0_ffn_Wu, np.float32),
                   np.asarray(L0_ffn_Wd, np.float32))
    eye = np.eye(128, dtype=np.float32)
    zeros_pre = np.zeros((TCORE, C), dtype=np.float32)

    in_maps = []
    for c in range(8):
        b, h = c >> 1, c & 1
        in_maps.append(dict(
            xr=np.ascontiguousarray(x[b, h * TCORE:(h + 1) * TCORE]),
            xp=(np.ascontiguousarray(x[b, :TCORE]) if h == 1 else zeros_pre),
            eye=eye,
            selv=np.full((128, 1), float(h), np.float32),
            nselv=np.full((128, 1), float(1 - h), np.float32),
            nselbig=np.full((128, 1), float(1 - h) * -1e30, np.float32),
            **w))

    nc = build_nc()
    res = run_bass_kernel_spmd(nc, in_maps, core_ids=list(range(8)))
    outp = np.empty((B, T, C), dtype=np.float32)
    for c in range(8):
        b, h = c >> 1, c & 1
        outp[b, h * TCORE:(h + 1) * TCORE] = res.results[c]["out"]
    return outp


# version 12
# speedup vs baseline: 1.0948x; 1.0948x over previous
import numpy as np
import concourse.bass as bass
import concourse.mybir as mybir
from concourse.bass_utils import run_bass_kernel_spmd

F32 = mybir.dt.float32
BF16 = mybir.dt.bfloat16
I32 = mybir.dt.int32
AF = mybir.ActivationFunctionType
ALU = mybir.AluOpType
AX = mybir.AxisListType

B, T, C = 4, 8192, 2048
KC = 16
K2 = 32
TCORE = 4096
ST = 1024
SUB = 512
NG = 8
NE = 8
NCD = 4
PI = float(np.pi)
INV2PI = float(1.0 / (2 * np.pi))
TWOPI = float(2 * np.pi)
EPS = 1e-5

ENGS = ("sync", "gpsimd", "tensor", "vector", "scalar")


def build_nc(sups=None, dumps=None, stop=None, xvdump=False, tdebug=False):
    nc = bass.Bass("TRN2", target_bir_lowering=False,
                   dynamic_dma_scratch_size=4096)

    xr = nc.dram_tensor("xr", [TCORE, C], F32, kind="ExternalInput")
    xp = nc.dram_tensor("xp", [TCORE, C], F32, kind="ExternalInput")
    apk = nc.dram_tensor("apk", [KC, 128, K2 * 128], F32, kind="ExternalInput")
    vrpk = nc.dram_tensor("vrpk", [KC, 128, (K2 + KC) * 128], F32, kind="ExternalInput")
    gupk = nc.dram_tensor("gupk", [NE, KC, 128, 2 * KC * 128], F32, kind="ExternalInput")
    dpk = nc.dram_tensor("dpk", [NE, NCD, 128, KC * 512], F32, kind="ExternalInput")
    wrk = nc.dram_tensor("wrk", [128, KC * 8], F32, kind="ExternalInput")
    eye = nc.dram_tensor("eye", [128, 128], F32, kind="ExternalInput")
    selv = nc.dram_tensor("selv", [128, 1], F32, kind="ExternalInput")
    nselv = nc.dram_tensor("nselv", [128, 1], F32, kind="ExternalInput")
    nselbigv = nc.dram_tensor("nselbig", [128, 1], F32, kind="ExternalInput")
    out = nc.dram_tensor("out", [TCORE, C], F32, kind="ExternalOutput")

    pad = nc.alloc_sbuf_tensor("ringpad", [128, 12288], mybir.dt.uint8)
    acc = nc.alloc_sbuf_tensor("acc", [128, NG * C], F32)
    xnT = nc.alloc_sbuf_tensor("xnT", [128, KC * (ST + 1)], BF16)
    xst = nc.alloc_sbuf_tensor("xst", [128, C], F32)
    kvp = nc.alloc_sbuf_tensor("kvp", [128, KC * ST], BF16)
    war = nc.alloc_sbuf_tensor("war", [128, 24576], BF16)
    scr = nc.alloc_sbuf_tensor("scr", [128, 4096], F32)
    wrs = nc.alloc_sbuf_tensor("wrs", [128, KC * 8], BF16)
    idb = nc.alloc_sbuf_tensor("idb", [128, 128], BF16)
    idf = nc.alloc_sbuf_tensor("idf", [128, 128], F32)
    wmap = nc.alloc_sbuf_tensor("wmap", [128, NG * NE], F32)
    lgt = nc.alloc_sbuf_tensor("lgt", [128, 16], F32)
    lgt8 = nc.alloc_sbuf_tensor("lgt8", [128, NG * 8], F32)
    carr = nc.alloc_sbuf_tensor("carr", [128, 24], F32)
    col = nc.alloc_sbuf_tensor("col", [128, 16], F32)
    sel = nc.alloc_sbuf_tensor("sel", [128, 1], F32)
    nsel = nc.alloc_sbuf_tensor("nsel", [128, 1], F32)
    nselbig = nc.alloc_sbuf_tensor("nselb", [128, 1], F32)
    scrb = scr.bitcast(BF16)
    scri = scr.bitcast(I32)

    def S(k):
        return scr[:, 512 * k:512 * (k + 1)]

    def Si(k):
        return scri[:, 512 * k:512 * (k + 1)]

    cneg1 = col[:, 0:1]
    chalf = col[:, 13:14]
    zc = col[:, 1:2]
    stdc = col[:, 2:3]
    rstdc = col[:, 3:4]
    meanc = col[:, 4:5]
    sumc = col[:, 5:6]
    sqc = col[:, 6:7]
    varc = col[:, 7:8]
    m1c = col[:, 8:9]
    m2c = col[:, 9:10]
    p1c = col[:, 10:11]
    p2c = col[:, 11:12]
    epsc = col[:, 12:13]

    pb = [nc.alloc_psum_tensor(f"pb{i}", [128, 512], F32) for i in range(4)]
    TB = [nc.alloc_psum_tensor(f"ptb{i}", [128, 128], BF16) for i in range(4)]

    def aslab(i):
        return war[:, 4096 * i:4096 * (i + 1)]

    def vrslab(i):
        return war[:, 12288 + 6144 * i:12288 + 6144 * (i + 1)]

    def guslab(i):
        return war[:, 4096 * i:4096 * (i + 1)]

    def dslab(i):
        return war[:, 8192 + 8192 * i:8192 + 8192 * (i + 1)]

    q = {e: [] for e in ENGS}
    cnt = {"XL": 0, "WL": 0, "TP": 0, "VE": 0, "SE": 0, "OS": 0, "GE": 0,
           "WA0": 0, "WA1": 0, "WV0": 0, "WV1": 0, "WG0": 0, "WG1": 0,
           "WD0": 0, "WD1": 0, "OSB": 0}
    lastw = {e: {s: 0 for s in cnt} for e in ENGS}
    SEQ = {}
    carry_inc = {}
    bankfree = {i: ("VE", 0) for i in range(4)}
    tbfree = {i: 0 for i in range(4)}

    def emit(eng, fn):
        q[eng].append(fn)

    def wait(eng, sname, val):
        if val <= 0 or lastw[eng][sname] >= val:
            return
        lastw[eng][sname] = val
        emit(eng, lambda e, sn=sname, v=val: e.wait_ge(SEM[sn], v))

    def step(eng, sname, fn, k=1):
        cnt[sname] += k
        emit(eng, lambda e, sn=sname, f=fn, kk=k: f(e).then_inc(SEM[sn], kk))
        return cnt[sname]

    def plain(eng, fn):
        emit(eng, fn)

    def wait_bank(eng, b):
        sname, v = bankfree[b]
        wait(eng, sname, v)

    v00 = step("vector", "VE", lambda e: e.memset(col.ap(), 0.0))
    wait("vector", "VE", v00)
    plain("vector", lambda e: e.memset(cneg1, -1.0))
    plain("vector", lambda e: e.memset(epsc, EPS))
    plain("vector", lambda e: e.memset(chalf, 0.5))
    plain("vector", lambda e: e.memset(xnT.ap(), 0.0))
    plain("vector", lambda e: e.memset(carr[:, 0:8], -1e30))
    plain("vector", lambda e: e.memset(carr[:, 8:16], 0.0))
    step("vector", "VE", lambda e: e.memset(carr[:, 16:24], 0.0))
    xa = step("sync", "XL", lambda e: e.dma_start(idf.ap(), eye.ap()), 16)
    wait("sync", "XL", xa)
    xa = step("sync", "XL", lambda e: e.dma_start(sel.ap(), selv.ap()), 16)
    wait("sync", "XL", xa)
    xa = step("sync", "XL", lambda e: e.dma_start(nsel.ap(), nselv.ap()), 16)
    wait("sync", "XL", xa)
    x0 = step("sync", "XL", lambda e: e.dma_start(nselbig.ap(), nselbigv.ap()), 16)
    w00 = step("gpsimd", "WL", lambda e: e.dma_start(idb.ap(), eye.ap()), 16)
    wait("gpsimd", "WL", w00)
    w0 = step("gpsimd", "WL", lambda e: e.dma_start(wrs.ap(), wrk.ap()), 16)
    wait("vector", "XL", x0)
    wait("vector", "WL", w0)
    wait("tensor", "WL", w0)
    wait("tensor", "XL", x0)

    def ln_phase(s):
        real = s >= 4
        xsrc = xr if real else xp
        tb = (s % 4) * ST
        xnst = scrb[:, 0:2048]
        for g in range(NG):
            wait("sync", "VE", SEQ.get("xfree_ve", 0))
            wait("sync", "SE", SEQ.get("xfree_se", 0))
            wait("sync", "XL", cnt["XL"])
            xl = step("sync", "XL", lambda e, o=(tb + g * 128) * C:
                      e.dma_start(xst.ap(),
                                  bass.AP(xsrc, o, [[C, 128], [1, C]])), 16)
            wait("vector", "XL", xl)
            vr = step("vector", "VE", lambda e: e.reduce_sum(sumc, xst.ap(), AX.X))
            wait("vector", "VE", vr)
            wait("scalar", "XL", xl)
            wait("scalar", "VE", SEQ.get("scr_free_ve", 0))
            wait("scalar", "GE", SEQ.get("scr_free_ge", 0))
            wait("scalar", "VE", SEQ.get("sq_free_ve", 0))
            wait("scalar", "TP", SEQ.get("xnst_free_tp", 0))
            se1 = step("scalar", "SE", lambda e, ss=xnst:
                       e.activation(ss, xst.ap(), AF.Square, accum_out=sqc))
            SEQ["xfree_se"] = se1
            wait("vector", "SE", se1)
            plain("vector", lambda e: e.tensor_scalar(meanc, sumc, 1.0 / C, None, ALU.mult))
            plain("vector", lambda e: e.tensor_scalar(varc, sqc, 1.0 / C, None, ALU.mult))
            vm = step("vector", "VE", lambda e: e.tensor_tensor(m1c, meanc, meanc, ALU.mult))
            wait("vector", "VE", vm)
            v1 = step("vector", "VE", lambda e:
                      e.tensor_tensor(varc, varc, m1c, ALU.subtract))
            SEQ["sq_free_ve"] = v1
            wait("scalar", "VE", v1)
            se2 = step("scalar", "SE", lambda e:
                       e.activation(stdc, varc, AF.Sqrt, bias=epsc))
            wait("vector", "SE", se2)
            vrc = step("vector", "VE", lambda e: e.reciprocal(rstdc, stdc))
            wait("vector", "VE", vrc)
            wait("vector", "TP", SEQ.get("xnst_free_tp", 0))
            vxn = step("vector", "VE", lambda e, ss=xnst:
                       e.tensor_scalar(ss, xst.ap(), meanc, rstdc,
                                       ALU.subtract, ALU.mult))
            if real:
                vac = step("vector", "VE", lambda e, gg=g:
                           e.tensor_copy(acc[:, gg * C:(gg + 1) * C], xst.ap()))
                SEQ["xfree_ve"] = vac
            else:
                SEQ["xfree_ve"] = vxn
            for k in range(KC):
                bk = k % 4
                wait("tensor", "VE", vxn)
                wait("tensor", "VE", tbfree[bk])
                tp = step("tensor", "TP", lambda e, kk=k, bb=bk, ss=xnst:
                          e.transpose(TB[bb].ap(),
                                      ss[:, kk * 128:(kk + 1) * 128], idb.ap()))
                wait("vector", "TP", tp)
                vf = step("vector", "VE", lambda e, kk=k, bb=bk, gg=g:
                          e.tensor_copy(
                              xnT[:, kk * (ST + 1) + 1 + gg * 128:
                                  kk * (ST + 1) + 1 + gg * 128 + 128],
                              TB[bb].ap()))
                tbfree[bk] = vf
            SEQ["xnst_free_tp"] = cnt["TP"]
            if g == 3:
                SEQ["xnT_half_ve"] = cnt["VE"]
        SEQ["xnT_ve"] = cnt["VE"]

    def halo_copy():
        src = bass.AP(xnT, ST, [[KC * (ST + 1), 128], [(ST + 1), KC]])
        dst = bass.AP(xnT, 0, [[KC * (ST + 1), 128], [(ST + 1), KC]])
        wait("vector", "TP", SEQ.get("xnT_lastuse_tp", 0))
        step("vector", "VE", lambda e, s_=src, d_=dst: e.tensor_copy(d_, s_))

    def attn_scan_phase(s):
        real = s >= 4
        mcs = []
        for p in range(8):
            mcs += [p, p + 8]
        def issue_slab(i):
            wait("gpsimd", "TP", max(SEQ.get(f"asl{i % 2}_free", 0),
                                     SEQ.get("GU_last", 0)))
            sn = f"WA{i % 2}"
            return (sn, step("gpsimd", sn, lambda e, m=mcs[i], b=i % 2:
                             e.dma_start(aslab(b), apk[m]), 16))

        wls = [None] * 16
        wls[0] = issue_slab(0)
        wls[1] = issue_slab(1)
        fi = 0
        for ip in range(8):
            p = ip
            for ih in range(2):
                i = 2 * ip + ih
                buf = i % 2
                for sub in range(2):
                    bank = fi % 4
                    wait("tensor", wls[i][0], wls[i][1])
                    wait("tensor", "VE",
                         SEQ["xnT_half_ve"] if sub == 0 else SEQ["xnT_ve"])
                    wait_bank("tensor", bank)

                    def fill(e, bk=bank, bf=buf, sb=sub):
                        ins = None
                        for k2 in range(K2):
                            blk = k2 % KC
                            off = 0 if k2 < KC else 1
                            rhs = xnT[:, blk * (ST + 1) + off + sb * SUB:
                                      blk * (ST + 1) + off + sb * SUB + SUB]
                            ins = e.matmul(pb[bk].ap(),
                                           aslab(bf)[:, k2 * 128:(k2 + 1) * 128],
                                           rhs, start=(k2 == 0), stop=(k2 == K2 - 1))
                        return ins
                    tpf = step("tensor", "TP", fill)
                    slot = sub * 2 + ih
                    wait("scalar", "TP", tpf)
                    wait("scalar", "VE", SEQ.get(f"slot{slot}_free", 0))
                    wait("scalar", "GE", SEQ.get(f"slot{slot}_free_ge", 0))
                    plain("scalar", lambda e, bk=bank, sl=slot:
                          e.activation(S(sl), pb[bk].ap(), AF.Relu))
                    ses = step("scalar", "SE", lambda e, sl=slot:
                               e.activation(S(sl), S(sl), AF.Sigmoid))
                    bankfree[bank] = ("SE", ses)
                    if xvdump:
                        wait("vector", "SE", ses)
                        mcv = mcs[i]
                        step("vector", "VE", lambda e, sl=slot, m=mcv, sb=sub:
                             e.tensor_copy(
                                 kvp[:, m * ST + sb * SUB:m * ST + sb * SUB + SUB],
                                 S(sl)))
                    SEQ[f"sig{slot}_se"] = ses
                    fi += 1
                SEQ[f"asl{buf}_free"] = cnt["TP"]
                if i + 2 < 16:
                    wls[i + 2] = issue_slab(i + 2)
            for sub in range(2):
                re_ = S(sub * 2 + 0)
                im_ = S(sub * 2 + 1)
                cm = carr[:, p:p + 1]
                cP = carr[:, 8 + p:8 + p + 1]
                cA = carr[:, 16 + p:16 + p + 1]
                wait("gpsimd", "SE", SEQ[f"sig{sub * 2}_se"])
                wait("gpsimd", "SE", SEQ[f"sig{sub * 2 + 1}_se"])
                wait("gpsimd", "VE", SEQ.get("s45_scan_ve", 0))
                plain("gpsimd", lambda e, r=re_: e.tensor_tensor(S(4), r, r, ALU.mult))
                plain("gpsimd", lambda e, m=im_: e.tensor_tensor(S(5), m, m, ALU.mult))
                g2m = step("gpsimd", "GE", lambda e:
                           e.tensor_tensor(S(4), S(4), S(5), ALU.add))
                wait("scalar", "GE", g2m)
                plain("scalar", lambda e, m=im_: e.activation(S(5), m, AF.Ln))
                seL = step("scalar", "SE", lambda e, r=re_: e.activation(r, r, AF.Ln))
                wait("gpsimd", "SE", seL)
                gdr = step("gpsimd", "GE", lambda e, r=re_:
                           e.tensor_tensor(S(5), S(5), r, ALU.subtract))
                se3 = step("scalar", "SE", lambda e: e.activation(S(6), S(4), AF.Ln))
                wait("scalar", "GE", gdr)
                seR = step("scalar", "SE", lambda e, m=im_:
                           e.activation(m, S(5), AF.Exp))
                wait("vector", "SE", se3)
                wait("vector", "VE", carry_inc.get(("m", p), 0))
                vs1 = step("vector", "VE", lambda e, c=cm:
                      e.tensor_tensor_scan(S(4), S(6), S(6), c, ALU.max, ALU.bypass))
                wait("vector", "VE", vs1)
                carry_inc[("m", p)] = step("vector", "VE", lambda e, c=cm:
                      e.tensor_copy(c, S(4)[:, SUB - 1:SUB]))
                v2 = step("vector", "VE", lambda e:
                          e.tensor_tensor(S(6), S(6), S(4), ALU.subtract))
                wait("vector", "VE", carry_inc.get(("P", p), 0))
                vs2 = step("vector", "VE", lambda e, c=cP:
                      e.tensor_tensor_scan(S(4), S(6), S(6), c, ALU.add, ALU.bypass))
                wait("vector", "VE", vs2)
                carry_inc[("P", p)] = step("vector", "VE", lambda e, c=cP:
                      e.tensor_copy(c, S(4)[:, SUB - 1:SUB]))
                if real and not xvdump:
                    wait("scalar", "VE", vs2)
                    seP = step("scalar", "SE", lambda e, r=re_:
                               e.activation(r, S(4), AF.Exp, scale=chalf))
                se4 = step("scalar", "SE", lambda e, m=im_:
                           e.activation(m, m, AF.Arctan))
                wait("vector", "SE", se4)
                wait("vector", "VE", carry_inc.get(("A", p), 0))
                vs3 = step("vector", "VE", lambda e, m=im_, c=cA:
                      e.tensor_tensor_scan(S(6), m, m, c, ALU.add, ALU.bypass))
                wait("vector", "VE", vs3)
                vA = step("vector", "VE", lambda e, c=cA:
                          e.tensor_copy(c, S(6)[:, SUB - 1:SUB]))
                carry_inc[("A", p)] = vA
                SEQ["s45_scan_ve"] = vA
                SEQ[f"slot{sub * 2}_free"] = vA
                SEQ[f"slot{sub * 2 + 1}_free"] = vA
                if real and not xvdump:
                    plain("vector", lambda e:
                          e.tensor_scalar(S(5), S(6), INV2PI, None, ALU.mult))
                    plain("vector", lambda e: e.tensor_copy(Si(7), S(5)))
                    v4 = step("vector", "VE", lambda e:
                              e.scalar_tensor_tensor(S(5), Si(7), TWOPI, S(6),
                                                     ALU.mult, ALU.subtract))
                    wait("scalar", "VE", v4)
                    se5 = step("scalar", "SE", lambda e:
                               e.activation(S(5), S(5), AF.Sin, scale=cneg1))
                    wait("gpsimd", "SE", max(se5, seP))
                    gki = step("gpsimd", "GE", lambda e, pp=p, sb=sub, r=re_:
                               e.tensor_tensor(
                                   kvp[:, (8 + pp) * ST + sb * SUB:
                                       (8 + pp) * ST + sb * SUB + SUB],
                                   r, S(5), ALU.mult))
                    wait("vector", "GE", gki)
                    plain("vector", lambda e:
                          e.tensor_scalar(S(6), S(6), PI / 2, None, ALU.add))
                    plain("vector", lambda e:
                          e.tensor_scalar(S(5), S(6), INV2PI, None, ALU.mult))
                    plain("vector", lambda e: e.tensor_copy(Si(7), S(5)))
                    v5 = step("vector", "VE", lambda e:
                              e.scalar_tensor_tensor(S(5), Si(7), TWOPI, S(6),
                                                     ALU.mult, ALU.subtract))
                    wait("scalar", "VE", v5)
                    se6 = step("scalar", "SE", lambda e:
                               e.activation(S(5), S(5), AF.Sin, scale=cneg1))
                    wait("gpsimd", "SE", se6)
                    gkr = step("gpsimd", "GE", lambda e, pp=p, sb=sub, r=re_:
                               e.tensor_tensor(
                                   kvp[:, pp * ST + sb * SUB:pp * ST + sb * SUB + SUB],
                                   r, S(5), ALU.mult))
                    SEQ["kv_ge"] = gkr
                    SEQ[f"slot{sub * 2}_free"] = cnt["VE"]
                    SEQ[f"slot{sub * 2}_free_ge"] = gkr
                    SEQ[f"slot{sub * 2 + 1}_free_ge"] = gkr
        if s >= 4:
            SEQ["vr_wl0"] = issue_vr(0)
            SEQ["vr_wl1"] = issue_vr(1)
        SEQ["kv_ve"] = cnt["VE"]
        SEQ["kv_ge_all"] = cnt["GE"]
        SEQ["scr_free_ve"] = cnt["VE"]
        SEQ["scr_free_ge"] = cnt["GE"]
        SEQ["A_last"] = cnt["TP"]
        SEQ["xnT_lastuse_tp"] = cnt["TP"]

    def carry_fixup():
        nbb = nselbig.broadcast_to((128, 8))
        wait("vector", "VE", max(carry_inc.values(), default=0))
        plain("vector", lambda e, n=nbb:
              e.scalar_tensor_tensor(carr[:, 0:8], carr[:, 0:8], sel.ap(), n,
                                     ALU.mult, ALU.add))
        plain("vector", lambda e:
              e.tensor_scalar(carr[:, 8:16], carr[:, 8:16], sel.ap(), None, ALU.mult))
        vcf = step("vector", "VE", lambda e:
              e.tensor_scalar(carr[:, 16:24], carr[:, 16:24], sel.ap(), None, ALU.mult))
        for k in list(carry_inc):
            carry_inc[k] = vcf

    def issue_vr(i):
        wait("gpsimd", "TP", max(SEQ.get(f"vrs{i % 2}_free", 0),
                                 SEQ.get("D_last", 0)))
        sn = f"WV{i % 2}"
        return (sn, step("gpsimd", sn, lambda e, m=i, b=i % 2:
                         e.dma_start(vrslab(b), vrpk[m]), 16))

    def attn_tail_phase(s):
        wl = [None] * KC
        wl[0] = SEQ["vr_wl0"]
        wl[1] = SEQ["vr_wl1"]
        pend = None

        def epilogue(bval, brr, sub, mc, tpr):
            wait("scalar", "TP", tpr)
            wait("scalar", "VE", SEQ.get("s45_free_ve", 0))
            wait("scalar", "GE", SEQ.get("scr_free_ge", 0))
            plain("scalar", lambda e, bk=brr: e.activation(S(4), pb[bk].ap(), AF.Relu))
            plain("scalar", lambda e: e.activation(S(4), S(4), AF.Sigmoid))
            plain("scalar", lambda e: e.activation(S(4), S(4), AF.Square))
            ser = step("scalar", "SE", lambda e: e.activation(S(4), S(4), AF.Square))
            bankfree[brr] = ("SE", ser)
            t2b = scrb[:, 5120:5632]
            wait("vector", "SE", ser)
            vt2 = step("vector", "VE", lambda e, bk=bval, tb2=t2b:
                       e.tensor_tensor(tb2, pb[bk].ap(), S(4), ALU.mult))
            bankfree[bval] = ("VE", vt2)
            SEQ["s45_free_ve"] = vt2
            tpts = []
            for tg in range(4):
                tb_ = tg % 4
                wait("tensor", "VE", vt2)
                wait("tensor", "VE", tbfree[tb_])
                tpt = step("tensor", "TP", lambda e, t=tg, b=tb_, tb2=t2b:
                           e.transpose(TB[b].ap(),
                                       tb2[:, t * 128:(t + 1) * 128], idb.ap()))
                tpts.append(tpt)
            wait("vector", "TP", tpts[-1])
            for tg in range(4):
                tb_ = tg % 4
                G = sub * 4 + tg
                vfa = step("vector", "VE", lambda e, b=tb_, gg=G, m=mc:
                           e.tensor_tensor(
                               acc[:, gg * C + m * 128:gg * C + m * 128 + 128],
                               acc[:, gg * C + m * 128:gg * C + m * 128 + 128],
                               TB[b].ap(), ALU.add))
                tbfree[tb_] = vfa

        for mc in range(KC):
            buf = mc % 2
            for sub in range(2):
                it = 2 * mc + sub
                bval = (2 * it) % 4
                brr = (2 * it + 1) % 4
                wait("tensor", wl[mc][0], wl[mc][1])
                wait("tensor", "VE", SEQ["kv_ve"])
                wait("tensor", "GE", SEQ.get("kv_ge_all", 0))
                wait_bank("tensor", bval)
                wait_bank("tensor", brr)

                def fillv(e, bf=buf, sb=sub, bk=bval):
                    ins = None
                    for kc in range(KC):
                        rhs = kvp[:, kc * ST + sb * SUB:kc * ST + sb * SUB + SUB]
                        ins = e.matmul(pb[bk].ap(),
                                       vrslab(bf)[:, (K2 + kc) * 128:(K2 + kc + 1) * 128],
                                       rhs, start=(kc == 0), stop=(kc == KC - 1))
                    return ins
                step("tensor", "TP", fillv)

                def fillr(e, bf=buf, sb=sub, bk=brr):
                    ins = None
                    for k2 in range(K2):
                        blk = k2 % KC
                        off = 0 if k2 < KC else 1
                        rhs = xnT[:, blk * (ST + 1) + off + sb * SUB:
                                  blk * (ST + 1) + off + sb * SUB + SUB]
                        ins = e.matmul(pb[bk].ap(),
                                       vrslab(bf)[:, k2 * 128:(k2 + 1) * 128],
                                       rhs, start=(k2 == 0), stop=(k2 == K2 - 1))
                    return ins
                tprr = step("tensor", "TP", fillr)
                if pend is not None:
                    epilogue(*pend)
                pend = (bval, brr, sub, mc, tprr)
            SEQ[f"vrs{buf}_free"] = cnt["TP"]
            if mc + 2 < KC:
                wl[mc + 2] = issue_vr(mc + 2)
        epilogue(*pend)
        SEQ["VR_last"] = cnt["TP"]
        SEQ["kvdead_tp"] = cnt["TP"]
        SEQ["scr_free_ve"] = cnt["VE"]

    def router_fill_phase(s):
        for tg in range(NG):
            bank = tg % 2
            wait("tensor", "VE", SEQ["xnT_ve"])
            wait_bank("tensor", bank)

            def fillrt(e, t=tg, bk=bank):
                ins = None
                for kc in range(KC):
                    lhsT = xnT[:, kc * (ST + 1) + 1 + t * 128:
                               kc * (ST + 1) + 1 + t * 128 + 128]
                    ins = e.matmul(pb[bk][:, 0:8], lhsT,
                                   wrs[:, kc * 8:(kc + 1) * 8],
                                   start=(kc == 0), stop=(kc == KC - 1))
                return ins
            tprt = step("tensor", "TP", fillrt)
            wait("vector", "TP", tprt)
            vl = step("vector", "VE", lambda e, bk=bank, t=tg:
                      e.tensor_copy(lgt8[:, t * 8:(t + 1) * 8], pb[bk][:, 0:8]))
            bankfree[bank] = ("VE", vl)
            SEQ["lgt8_ve"] = vl

    def router_top2_phase(s):
        wait("vector", "VE", SEQ["lgt8_ve"])
        for tg in range(NG):
            g1 = step("vector", "VE", lambda e, t=tg:
                      e.reduce_max(m1c, lgt8[:, t * 8:(t + 1) * 8], AX.X))
            wait("vector", "VE", g1)
            g2 = step("vector", "VE", lambda e, t=tg:
                  e.tensor_scalar(lgt[:, 8:16], lgt8[:, t * 8:(t + 1) * 8], m1c, None, ALU.is_ge))
            wait("vector", "VE", g2)
            g3 = step("vector", "VE", lambda e, t=tg:
                  e.scalar_tensor_tensor(S(4)[:, 0:8], lgt[:, 8:16], -1e30,
                                         lgt8[:, t * 8:(t + 1) * 8], ALU.mult, ALU.add))
            wait("vector", "VE", g3)
            g4 = step("vector", "VE", lambda e: e.reduce_max(m2c, S(4)[:, 0:8], AX.X))
            wait("vector", "VE", g4)
            g5 = step("vector", "VE", lambda e:
                  e.tensor_scalar(S(4)[:, 8:16], S(4)[:, 0:8], m2c, None, ALU.is_ge))
            v6 = step("vector", "VE", lambda e:
                      e.tensor_tensor(m2c, m2c, m1c, ALU.subtract))
            wait("scalar", "VE", v6)
            se7 = step("scalar", "SE", lambda e:
                       e.activation(p2c, m2c, AF.Sigmoid))
            wait("vector", "SE", se7)
            g6 = step("vector", "VE", lambda e:
                  e.tensor_scalar(p1c, p2c, -1.0, 1.0, ALU.mult, ALU.add))
            wait("vector", "VE", g6)
            g7 = step("vector", "VE", lambda e:
                  e.tensor_scalar(lgt[:, 8:16], lgt[:, 8:16], p1c, None, ALU.mult))
            wait("vector", "VE", max(g5, g7))
            step("vector", "VE", lambda e, t=tg:
                 e.scalar_tensor_tensor(wmap[:, t * NE:(t + 1) * NE], S(4)[:, 8:16],
                                        p2c, lgt[:, 8:16], ALU.mult, ALU.add))
        SEQ["wmap_ve"] = cnt["VE"]
        SEQ["scr_free_ve"] = cnt["VE"]

    def ffn_phase(s):
        for e_ in range(NE):
            def issue_gu(i, ee):
                wait("gpsimd", "TP", max(SEQ.get(f"gus{i % 2}_free", 0),
                                         SEQ.get("A_last", 0)))
                sn = f"WG{i % 2}"
                return (sn, step("gpsimd", sn, lambda en, m=i, b=i % 2, x=ee:
                                 en.dma_start(guslab(b), gupk[x, m]), 16))

            wl = [None] * KC
            wl[0] = issue_gu(0, e_)
            wl[1] = issue_gu(1, e_)
            for mc in range(KC):
                buf = mc % 2
                for half in range(2):
                    bg = 2 * half
                    bu = 2 * half + 1
                    wait("tensor", wl[mc][0], wl[mc][1])
                    wait_bank("tensor", bg)
                    wait_bank("tensor", bu)

                    def fillgu(en, bf=buf, hh=half, bk=bg, off=0):
                        ins = None
                        for kc in range(KC):
                            rhs = xnT[:, kc * (ST + 1) + 1 + hh * SUB:
                                      kc * (ST + 1) + 1 + hh * SUB + SUB]
                            ins = en.matmul(pb[bk].ap(),
                                            guslab(bf)[:, off + kc * 128:
                                                        off + (kc + 1) * 128],
                                            rhs, start=(kc == 0), stop=(kc == KC - 1))
                        return ins
                    step("tensor", "TP",
                         lambda en, bf=buf, hh=half, bk=bg: fillgu(en, bf, hh, bk, 0))
                    tpu = step("tensor", "TP",
                               lambda en, bf=buf, hh=half, bk=bu:
                               fillgu(en, bf, hh, bk, KC * 128))
                    wait("scalar", "TP", tpu)
                    wait("scalar", "VE", SEQ.get("s45_free_ve", 0))
                    wait("scalar", "GE", SEQ.get("scr_free_ge", 0))
                    seg = step("scalar", "SE", lambda en, bk=bg:
                               en.activation(S(4), pb[bk].ap(), AF.Silu))
                    bankfree[bg] = ("SE", seg)
                    wait("vector", "SE", seg)
                    if e_ == 0 and mc == 0 and half == 0:
                        wait("vector", "TP", SEQ.get("kvdead_tp", 0))
                    plain("vector", lambda en:
                          en.tensor_scalar(S(4), S(4), 7.0, None, ALU.min))
                    vp = step("vector", "VE", lambda en, m=mc, hh=half, bk=bu:
                              en.tensor_tensor(
                                  kvp[:, m * ST + hh * SUB:m * ST + hh * SUB + SUB],
                                  S(4), pb[bk].ap(), ALU.mult))
                    bankfree[bu] = ("VE", vp)
                    SEQ["s45_free_ve"] = vp
                SEQ[f"gus{buf}_free"] = cnt["TP"]
                if mc + 2 < KC:
                    wl[mc + 2] = issue_gu(mc + 2, e_)
            SEQ["pT_ve"] = cnt["VE"]
            SEQ["GU_last"] = cnt["TP"]

            def issue_d(i, ee):
                wait("gpsimd", "TP",
                     max(SEQ.get("VR_last", 0), SEQ.get(f"ds{i % 2}_free", 0)))
                sn = f"WD{i % 2}"
                return (sn, step("gpsimd", sn, lambda en, m=i, b=i % 2, x=ee:
                                 en.dma_start(dslab(b), dpk[x, m]), 16))

            wld = [None] * NCD
            wld[0] = issue_d(0, e_)
            wld[1] = issue_d(1, e_)
            for ncd in range(NCD):
                buf = ncd % 2
                for tg in range(NG):
                    bank = (ncd * NG + tg) % 4
                    wait("tensor", wld[ncd][0], wld[ncd][1])
                    wait("tensor", "VE", SEQ["pT_ve"])
                    wait_bank("tensor", bank)

                    def filld(en, t=tg, bf=buf, bk=bank):
                        ins = None
                        for kc in range(KC):
                            lhsT = kvp[:, kc * ST + t * 128:kc * ST + t * 128 + 128]
                            ins = en.matmul(pb[bk].ap(), lhsT,
                                            dslab(bf)[:, kc * 512:(kc + 1) * 512],
                                            start=(kc == 0), stop=(kc == KC - 1))
                        return ins
                    tpd = step("tensor", "TP", filld)
                    wait("vector", "TP", tpd)
                    wcol = wmap[:, tg * NE + e_:tg * NE + e_ + 1]
                    ao = tg * C + ncd * 512
                    vd = step("vector", "VE", lambda en, bk=bank, w=wcol, a=ao:
                              en.scalar_tensor_tensor(
                                  acc[:, a:a + 512], pb[bk].ap(), w,
                                  acc[:, a:a + 512], ALU.mult, ALU.add))
                    bankfree[bank] = ("VE", vd)
                SEQ[f"ds{buf}_free"] = cnt["TP"]
                if ncd + 2 < NCD:
                    wld[ncd + 2] = issue_d(ncd + 2, e_)
            SEQ["D_last"] = cnt["TP"]
        SEQ["acc_ve"] = cnt["VE"]
        SEQ["xnT_lastuse_tp"] = cnt["TP"]
        SEQ["scr_free_ve"] = cnt["VE"]

    def store_phase(s):
        tb = (s - 4) * ST
        wait("sync", "VE", SEQ["acc_ve"])
        sn = "OS" if (s % 2 == 0) else "OSB"
        os = step("sync", sn, lambda e:
                  e.dma_start(bass.AP(out, tb * C, [[C, 128], [128 * C, NG], [1, C]]),
                              acc.ap()), 16)
        SEQ["store_os"] = (sn, os)

    if sups is None:
        sups = list(range(8))
    first = True
    for s in sups:
        if not first:
            halo_copy()
        first = False
        if s >= 4 and "store_os" in SEQ:
            wait("vector", SEQ["store_os"][0], SEQ["store_os"][1])
        ln_phase(s)
        if stop == "ln":
            continue
        if s >= 4:
            router_fill_phase(s)
        attn_scan_phase(s)
        if s == 3:
            carry_fixup()
        if stop == "scan":
            continue
        if s >= 4:
            attn_tail_phase(s)
            if stop == "tail":
                continue
            router_top2_phase(s)
            if stop == "router":
                continue
            ffn_phase(s)
            store_phase(s)
    if dumps:
        dmap = {"xnT": xnT, "acc": acc, "kvp": kvp, "wmap": wmap,
                "carr": carr, "scr": scr, "col": col, "lgt": lgt}
        step("vector", "VE", lambda e: e.memset(col[:, 15:16], 0.0))
        step("scalar", "SE", lambda e: e.activation(col[:, 14:15], col[:, 15:16], AF.Square))
        wait("sync", "VE", cnt["VE"])
        wait("sync", "SE", cnt["SE"])
        wait("sync", "TP", cnt["TP"])
        for nm in dumps:
            h = dmap[nm]
            d = nc.dram_tensor("d_" + nm, list(h.shape), h.dtype, kind="ExternalOutput")
            step("sync", "OS", lambda e, dd=d, a=h.ap(): e.dma_start(dd.ap(), a), 16)
    wait("sync", "OS", cnt["OS"])

    global SEM
    with (
        nc.semaphore("sXL") as s_xl,
        nc.semaphore("sWL") as s_wl,
        nc.semaphore("sTP") as s_tp,
        nc.semaphore("sVE") as s_ve,
        nc.semaphore("sSE") as s_se,
        nc.semaphore("sOS") as s_os,
        nc.semaphore("sGE") as s_ge,
        nc.semaphore("sWA0") as s_wa0,
        nc.semaphore("sWA1") as s_wa1,
        nc.semaphore("sWV0") as s_wv0,
        nc.semaphore("sWV1") as s_wv1,
        nc.semaphore("sWG0") as s_wg0,
        nc.semaphore("sWG1") as s_wg1,
        nc.semaphore("sWD0") as s_wd0,
        nc.semaphore("sWD1") as s_wd1,
        nc.semaphore("sOSB") as s_osb,
        nc.Block() as block,
    ):
        SEM = {"XL": s_xl, "WL": s_wl, "TP": s_tp, "VE": s_ve, "SE": s_se, "OS": s_os,
               "GE": s_ge, "WA0": s_wa0, "WA1": s_wa1, "WV0": s_wv0, "WV1": s_wv1,
               "WG0": s_wg0, "WG1": s_wg1, "WD0": s_wd0, "WD1": s_wd1, "OSB": s_osb}

        @block.sync
        def _(e):
            for fn in q["sync"]:
                fn(e)

        @block.gpsimd
        def _(e):
            for fn in q["gpsimd"]:
                fn(e)

        @block.tensor
        def _(e):
            for fn in q["tensor"]:
                fn(e)

        @block.vector
        def _(e):
            for fn in q["vector"]:
                fn(e)

        @block.scalar
        def _(e):
            for fn in q["scalar"]:
                fn(e)

    return nc


def _host_prep(W_act, W_val, W_rec, Wr, Wg, Wu, Wd):
    perm = np.concatenate([np.arange(0, C, 2), np.arange(1, C, 2)])

    def mcol_pack(w):
        K = w.shape[0]
        nmc = w.shape[1] // 128
        r = w.reshape(K // 128, 128, nmc, 128)
        r = r.transpose(2, 1, 0, 3).reshape(nmc, 128, K)
        return np.ascontiguousarray(r)

    A = mcol_pack(np.ascontiguousarray(W_act[perm, :].T))
    R = mcol_pack(np.ascontiguousarray(W_rec.T))
    V = mcol_pack(np.ascontiguousarray(W_val[:, perm].T))
    vr = np.concatenate([R, V], axis=2)

    G = np.stack([mcol_pack(Wg[e]) for e in range(NE)])
    U = np.stack([mcol_pack(Wu[e]) for e in range(NE)])
    gu = np.concatenate([G, U], axis=3)

    Wd5 = Wd.reshape(NE, KC, 128, NCD, 512)
    dp = np.ascontiguousarray(
        Wd5.transpose(0, 3, 2, 1, 4).reshape(NE, NCD, 128, KC * 512))

    wr = np.ascontiguousarray(
        Wr.reshape(KC, 128, NE).transpose(1, 0, 2).reshape(128, KC * NE))

    return dict(apk=A, vrpk=vr, gupk=gu, dpk=dp, wrk=wr)


def kernel(L0_attn_x, L0_attn_W_act, L0_attn_W_val, L0_attn_W_rec,
           L0_ffn_Wr, L0_ffn_Wg, L0_ffn_Wu, L0_ffn_Wd):
    x = np.ascontiguousarray(np.asarray(L0_attn_x, dtype=np.float32))
    w = _host_prep(np.asarray(L0_attn_W_act, np.float32),
                   np.asarray(L0_attn_W_val, np.float32),
                   np.asarray(L0_attn_W_rec, np.float32),
                   np.asarray(L0_ffn_Wr, np.float32),
                   np.asarray(L0_ffn_Wg, np.float32),
                   np.asarray(L0_ffn_Wu, np.float32),
                   np.asarray(L0_ffn_Wd, np.float32))
    eye = np.eye(128, dtype=np.float32)
    zeros_pre = np.zeros((TCORE, C), dtype=np.float32)

    in_maps = []
    for c in range(8):
        b, h = c >> 1, c & 1
        in_maps.append(dict(
            xr=np.ascontiguousarray(x[b, h * TCORE:(h + 1) * TCORE]),
            xp=(np.ascontiguousarray(x[b, :TCORE]) if h == 1 else zeros_pre),
            eye=eye,
            selv=np.full((128, 1), float(h), np.float32),
            nselv=np.full((128, 1), float(1 - h), np.float32),
            nselbig=np.full((128, 1), float(1 - h) * -1e30, np.float32),
            **w))

    nc = build_nc()
    res = run_bass_kernel_spmd(nc, in_maps, core_ids=list(range(8)))
    outp = np.empty((B, T, C), dtype=np.float32)
    for c in range(8):
        b, h = c >> 1, c & 1
        outp[b, h * TCORE:(h + 1) * TCORE] = res.results[c]["out"]
    return outp
